# version 03
# baseline (speedup 1.0000x reference)
import numpy as np
import concourse.bass as bass
import concourse.mybir as mybir
from concourse.bass_utils import run_bass_kernel_spmd
from concourse.alu_op_type import AluOpType as OP

F32 = mybir.dt.float32
BF16 = mybir.dt.bfloat16
I32 = mybir.dt.int32
U8 = mybir.dt.uint8
AF = mybir.ActivationFunctionType
AXX = mybir.AxisListType.X

N_CORES = 8
T, D = 2048, 1024
NSEQ = 2
H, HD = 16, 64
C0 = 128
NC0 = T // C0
DH = 4096
DI, DS, DCONV, DTR = 2048, 16, 4, 64
TS = 256
NCH = T // TS
TT = 512
NTT = T // TT
EXP_NEG10 = float(np.float32(np.exp(np.float64(-10.0))))

BLOCKS = ("l0a", "ln", "l0f", "ln", "l1a", "ln", "l1f", "ln")
DEBUG_CAP = None
CARVE_LOG = []


class Bld:
    def __init__(self):
        self.nc = bass.Bass("TRN2", target_bir_lowering=False,
                            detect_race_conditions=False)
        self.sems, self.semval = {}, {}
        self.stream = {e: [] for e in ("sync", "scalar", "vector", "gpsimd", "tensor")}
        self.ps = [self.nc.alloc_psum_tensor(f"ps{i}", [128, 512], F32).ap()
                   for i in range(6)]
        self.pbig = self.nc.alloc_psum_tensor("pbig", [128, 1024], F32).ap()
        self.ps.append(self.pbig[:, 0:512])
        self.ps.append(self.pbig[:, 512:1024])
        self.bank_free = [("none", 0)] * 8
        self._dram()
        self._sbuf()

    def sem(self, name):
        if name not in self.sems:
            self.sems[name] = self.nc.semaphore(name).__enter__()
            self.semval[name] = 0
        return self.sems[name]

    def emit(self, eng, fn, waits=(), inc=None, inc_by=1):
        s = self.sems
        waits = tuple((w, v) for (w, v) in waits if w != "none" and v > 0)
        for w, _ in waits:
            self.sem(w)
        if inc is not None:
            self.sem(inc)
            self.semval[inc] += inc_by

        def closure(h, fn=fn, waits=waits, inc=inc, inc_by=inc_by):
            for wname, wval in waits:
                h.wait_ge(s[wname], wval)
            inst = fn(h)
            if inc is not None:
                inst.then_inc(s[inc], inc_by)
        self.stream[eng].append(closure)
        return self.semval[inc] if inc is not None else None

    def dma(self, eng, out, in_, waits=(), sem=None):
        name = sem or ("dma_" + eng)
        if not hasattr(self, "dma_semset"):
            self.dma_semset = set()
        self.dma_semset.add(name)
        return self.emit(eng, lambda h, o=out, i=in_: h.dma_start(out=o, in_=i),
                         waits=waits, inc=name, inc_by=16)

    def val(self, name):
        return self.semval.get(name, 0)

    def barrier(self):
        engs = ("scalar", "vector", "gpsimd", "tensor")
        vals = {}
        fc = self.c["fconst"]
        for e in engs:
            n = "bar_" + e
            self.sem(n)
            if e == "scalar":
                fn = lambda h: h.copy(out=fc[0:1, 8:9], in_=fc[0:1, 3:4])
            elif e == "vector":
                fn = lambda h: h.tensor_copy(out=fc[0:1, 9:10], in_=fc[0:1, 3:4])
            elif e == "gpsimd":
                fn = lambda h: h.tensor_copy(out=fc[0:1, 10:11], in_=fc[0:1, 3:4])
            else:
                fn = lambda h: h.transpose(self.ps[0][0:1, 0:1], self.id_t[0:1, 0:1],
                                           self.id_t[0:1, 0:1])
            w = [self.bank_free[0]] if e == "tensor" else ()
            vals[e] = self.emit(e, fn, waits=w, inc=n)
        names = set(getattr(self, "dma_semset", set()))
        names.update("dma_" + q for q in ("sync", "scalar", "vector", "gpsimd"))
        dmav = {n: self.val(n) for n in names if self.val(n)}
        for n in dmav:
            self.sem(n)
        for e in engs + ("sync",):
            def wait_fn(h, vals=dict(vals), dmav=dict(dmav), s=self.sems):
                for oe, v in vals.items():
                    h.wait_ge(s["bar_" + oe], v)
                for n, v in dmav.items():
                    h.wait_ge(s[n], v)
            self.stream[e].append(wait_fn)
        self.bank_free = [("none", 0)] * 8

    def finalize(self):
        with self.nc.Block() as block:
            for eng, dec in (("sync", block.sync), ("scalar", block.scalar),
                             ("vector", block.vector), ("gpsimd", block.gpsimd),
                             ("tensor", block.tensor)):
                ops = self.stream[eng]

                def section(h, ops=ops):
                    for fn in ops:
                        fn(h)
                dec(section)
        return self.nc

    def _dram(self):
        nc = self.nc
        self.din = {}

        def din(name, shape, dtype=F32):
            ap = nc.dram_tensor(name, shape, dtype, kind="ExternalInput").ap()
            self.din[name] = ap
            return ap
        self.x_in = din("x", [NSEQ, D, T])
        self.out_d = nc.dram_tensor("out", [NSEQ, D, T], F32, kind="ExternalOutput").ap()
        self.WwT = din("WwT", [D, D])
        self.WrT_t = din("WrT_t", [8, 8, 128, 128])
        din("decay", [1, H])
        din("log_alpha", [1, H])
        self.w1T = [din("ffn0_w1T", [D, DH]), din("ffn1_w1T", [D, DH])]
        self.w2T = [din("ffn0_w2T", [DH, D]), din("ffn1_w2T", [DH, D])]
        self.fcw = [din("ffn0_cw", [DH, 3]), din("ffn1_cw", [DH, 3])]
        self.fcb = [din("ffn0_cb", [DH, 1]), din("ffn1_cb", [DH, 1])]
        self.WinT_t = din("WinT_t", [32, 8, 128, 128])
        din("m_cw", [DI, DCONV])
        din("m_cb", [DI, 1])
        self.WxT = din("WxT", [DI, DTR + 2 * DS])
        self.WdtT_t = din("WdtT_t", [16, 64, 128])
        din("b_dt", [DI, 1])
        din("A_log", [DI, DS])
        din("D_vec", [DI, 1])
        din("rms_w", [DI, 1])
        self.WoT_t = din("WoT_t", [16, 8, 128, 128])
        din("identity", [128, 128])
        din("cramp", [128, C0])
        din("crampr", [128, HD])
        din("diffsT", [C0, C0])
        din("causT", [C0, C0])

        def scr(name, shape, dtype=F32):
            return nc.dram_tensor(name, shape, dtype, kind="Internal").ap()
        self.bc_row = scr("bc_row", [1, 8192])
        self.hrow = scr("hrow", [1, 96])
        self.wq1_s = [scr(f"wq1s{l}", [32, 8, 128, 128], BF16) for l in (0, 1)]
        self.wq2_s = [scr(f"wq2s{l}", [8, 32, 128, 128], BF16) for l in (0, 1)]
        self.reads_s = scr("reads_s", [D, T])
        self.xc_s = scr("xc_s", [DI, T])
        self.sz_s = scr("sz_s", [DI, T])
        self.dtm_s = scr("dtm_s", [DI, T])
        self.yp_s = scr("yp_s", [DI, T])
        self.brep_s = scr("brep_s", [1, T * DS])
        self.crep_s = scr("crep_s", [1, T * DS])

    def _sbuf(self):
        nc = self.nc
        self.xr = [nc.alloc_sbuf_tensor(f"xr{i}", [128, T], F32).ap() for i in range(8)]
        self.id_t = nc.alloc_sbuf_tensor("id_t", [128, 128], F32).ap()
        self.c = {}

        def cal(name, shape, dtype=F32):
            self.c[name] = nc.alloc_sbuf_tensor("c_" + name, shape, dtype).ap()
        cal("gamrow", [1, 64])
        cal("A", [128, 16 * DS])
        cal("m_cw", [128, 16 * DCONV])
        cal("m_cb", [128, 16])
        cal("b_dt", [128, 16])
        cal("D_col", [128, 16])
        cal("rmsw", [128, 16])
        cal("f_cw", [128, 2 * 32 * 3])
        cal("f_cb", [128, 2 * 32])
        cal("wcol", [128, 12])
        cal("wrow", [1, 16])
        cal("sstat", [128, 64])
        cal("scanst", [128, 16 * DS])
        cal("cramp", [128, C0])
        cal("crampr", [128, HD])
        cal("bstage", [16, 128])
        cal("fconst", [128, 12])
        self.FC = {float(C0): 0, -0.5: 1, -1.0: 2, 1.0: 3, 1e-5: 4,
                   1.0 / DI: 5, 1e-6: 6, 0.0: 7}
        self.ARENA = 103 * 1024
        self.arena_t = nc.alloc_sbuf_tensor("arena", [128, self.ARENA], U8)
        self.WARENA = 28 * 1024
        self.warena_t = nc.alloc_sbuf_tensor("warena", [128, self.WARENA], U8)
        self._aoff = 0
        self._woff = 0

    def carve(self, shape, dtype=F32, which="a", at=None):
        lim = self.ARENA if which == "a" else self.WARENA
        t = self.arena_t if which == "a" else self.warena_t
        if at is None:
            off = self._aoff if which == "a" else self._woff
            off = (off + 63) // 64 * 64
        else:
            off = at
        n = int(np.prod(shape[1:])) * mybir.dt.size(dtype)
        import sys as _sys
        CARVE_LOG.append((_sys._getframe(1).f_lineno, which, off, list(shape),
                          str(dtype)))
        assert off + n <= lim, f"arena-{which} overflow {off + n} > {lim}"
        if at is None:
            if which == "a":
                self._aoff = off + n
            else:
                self._woff = off + n
        ap = t.ap()[:shape[0], off:off + n].bitcast(dtype)
        if len(shape) == 3:
            ap = ap.rearrange("p (a b) -> p a b", b=shape[2])
        return ap

    def mark(self, which="a"):
        return self._aoff if which == "a" else self._woff

    def reset_arena(self, which="a", to=0):
        if which == "a":
            self._aoff = to
        else:
            self._woff = to

    def free_bank(self, i, sem, val):
        self.bank_free[i] = (sem, val)

    def bcast_cols_to_row(self, cols_ap, n_blocks, bank, row_off, waits=None):
        if waits is None:
            waits = [("sv", self.val("sv"))]
        tv = self.emit("tensor", lambda h: h.transpose(
            self.ps[bank][0:n_blocks, 0:128], cols_ap, self.id_t),
            waits=list(waits) + [self.bank_free[bank]], inc="st_")
        stage = self.c["bstage"][0:n_blocks, :]
        ev = self.emit("vector", lambda h: h.tensor_copy(
            out=stage, in_=self.ps[bank][0:n_blocks, 0:128]),
            waits=[("st_", tv), ("bcw", self.val("bcw"))], inc="sv")
        self.free_bank(bank, "sv", ev)
        dst = bass.AP(self.bc_row.tensor, row_off, [[128, n_blocks], [1, 128]])
        return self.dma("gpsimd", dst, stage, waits=[("sv", ev)], sem="bcw")

    def bcast_read(self, dst_tile, row_off, n_elems, wv):
        src = bass.AP(self.bc_row.tensor, row_off, [[0, 128], [1, n_elems]])
        return self.dma("sync", dst_tile, src, waits=[("bcw", wv)], sem="bcr")

    def wcol(self, idx):
        return self.c["wcol"][:, idx:idx + 1]

    def fc(self, v):
        return self.c["fconst"][:, self.FC[float(v)]:self.FC[float(v)] + 1]

    def init_consts(self):
        for v, i in self.FC.items():
            self.emit("vector", lambda h, v=v, i=i: h.memset(
                self.c["fconst"][:, i:i + 1], v), inc="cst")
        cv = self.val("cst")
        for e in ("scalar", "gpsimd", "tensor", "sync"):
            def wait_fn(h, cv=cv, s=self.sems):
                h.wait_ge(s["cst"], cv)
            self.stream[e].append(wait_fn)


def host_layout(inputs):
    x = np.asarray(inputs["L0_attn_out"])
    shards = [np.ascontiguousarray(x[2 * ci:2 * ci + 2].transpose(0, 2, 1))
              for ci in range(N_CORES)]

    def tile4(w, a, bb):
        return np.ascontiguousarray(w.reshape(a, 128, bb, 128).transpose(0, 2, 1, 3))
    WrT = np.ascontiguousarray(inputs["L0_attn_W_read"].T)
    WinT = np.ascontiguousarray(inputs["L1_attn_W_in"].T)
    WdtT = np.ascontiguousarray(inputs["L1_attn_W_dt"].T)
    WoT = np.ascontiguousarray(inputs["L1_attn_W_out"].T)
    common = dict(
        WwT=np.ascontiguousarray(inputs["L0_attn_W_write"].T),
        WrT_t=tile4(WrT, 8, 8),
        decay=np.ascontiguousarray(np.asarray(inputs["L0_attn_decay"]).reshape(1, H)),
        log_alpha=np.ascontiguousarray(np.asarray(inputs["L0_attn_log_alpha"]).reshape(1, H)),
        ffn0_w1T=np.ascontiguousarray(inputs["L0_ffn_w1"].T),
        ffn0_w2T=np.ascontiguousarray(inputs["L0_ffn_w2"].T),
        ffn0_cw=np.ascontiguousarray(np.asarray(inputs["L0_ffn_conv_w"]).reshape(DH, 3)),
        ffn0_cb=np.ascontiguousarray(np.asarray(inputs["L0_ffn_conv_b"]).reshape(DH, 1)),
        ffn1_w1T=np.ascontiguousarray(inputs["L1_ffn_w1"].T),
        ffn1_w2T=np.ascontiguousarray(inputs["L1_ffn_w2"].T),
        ffn1_cw=np.ascontiguousarray(np.asarray(inputs["L1_ffn_conv_w"]).reshape(DH, 3)),
        ffn1_cb=np.ascontiguousarray(np.asarray(inputs["L1_ffn_conv_b"]).reshape(DH, 1)),
        WinT_t=np.ascontiguousarray(WinT.reshape(8, 128, 32, 128).transpose(2, 0, 1, 3)),
        m_cw=np.ascontiguousarray(inputs["L1_attn_W_conv"]),
        m_cb=np.ascontiguousarray(np.asarray(inputs["L1_attn_b_conv"]).reshape(DI, 1)),
        WxT=np.ascontiguousarray(inputs["L1_attn_W_x"].T),
        WdtT_t=np.ascontiguousarray(WdtT.reshape(64, 16, 128).transpose(1, 0, 2)),
        b_dt=np.ascontiguousarray(np.asarray(inputs["L1_attn_b_dt"]).reshape(DI, 1)),
        A_log=np.ascontiguousarray(inputs["L1_attn_A_log"]),
        D_vec=np.ascontiguousarray(np.asarray(inputs["L1_attn_D"]).reshape(DI, 1)),
        rms_w=np.ascontiguousarray(np.asarray(inputs["L1_attn_rms_w"]).reshape(DI, 1)),
        WoT_t=tile4(WoT, 16, 8),
        identity=np.eye(128, dtype=np.float32),
        diffsT=np.clip(np.arange(C0)[None, :] - 1 - np.arange(C0)[:, None],
                       0, None).astype(np.float32),
        causT=(np.arange(C0)[None, :] > np.arange(C0)[:, None]).astype(np.float32),
        cramp=np.broadcast_to(np.arange(C0, dtype=np.float32)[None, :],
                              (128, C0)).copy(),
        crampr=np.broadcast_to((C0 - 1 - np.arange(128, dtype=np.float32))[:, None],
                               (128, HD)).copy(),
    )
    return shards, common


def ph_prelude(b: Bld):
    c = b.c
    b.dma("sync", b.id_t, b.din["identity"], sem="c0")
    b.dma("sync", c["cramp"], b.din["cramp"], sem="c0")
    b.dma("sync", c["crampr"], b.din["crampr"], sem="c0")
    b.dma("sync", c["gamrow"][0:1, 0:16], b.din["decay"], sem="c0")
    b.dma("sync", c["gamrow"][0:1, 48:64], b.din["log_alpha"], sem="c0")
    A3 = b.din["A_log"].rearrange("(k p) n -> k p n", p=128)
    for k in range(16):
        b.dma("sync", c["A"][:, k * DS:(k + 1) * DS], A3[k], sem="c0")

    def ld_cols(dst, src, n, coff=0):
        s3 = src.rearrange("(k p) o -> k p o", p=128)
        for k in range(n):
            b.dma("sync", dst[:, coff + k:coff + k + 1], s3[k], sem="c0")
    ld_cols(c["m_cb"], b.din["m_cb"], 16)
    ld_cols(c["b_dt"], b.din["b_dt"], 16)
    ld_cols(c["D_col"], b.din["D_vec"], 16)
    ld_cols(c["rmsw"], b.din["rms_w"], 16)
    m3 = b.din["m_cw"].rearrange("(k p) o -> k p o", p=128)
    for k in range(16):
        b.dma("sync", c["m_cw"][:, k * DCONV:(k + 1) * DCONV], m3[k], sem="c0")
    for l in (0, 1):
        f3 = b.fcw[l].rearrange("(k p) o -> k p o", p=128)
        for k in range(32):
            b.dma("sync", c["f_cw"][:, (l * 32 + k) * 3:(l * 32 + k + 1) * 3], f3[k], sem="c0")
        ld_cols(c["f_cb"], b.fcb[l], 32, coff=l * 32)
    ldv = b.val("c0")

    g = c["gamrow"]
    b.emit("scalar", lambda h: h.activation(out=g[0:1, 0:16], in_=g[0:1, 0:16],
                                            func=AF.Sigmoid), waits=[("c0", ldv)])
    b.emit("scalar", lambda h: h.activation(out=g[0:1, 16:32], in_=g[0:1, 0:16],
                                            func=AF.Ln))
    b.emit("scalar", lambda h: h.activation(out=g[0:1, 32:48], in_=g[0:1, 16:32],
                                            func=AF.Exp, scale=float(C0)))
    sv = b.emit("scalar", lambda h: h.activation(out=g[0:1, 48:64], in_=g[0:1, 48:64],
                                                 func=AF.Exp), inc="ss")
    b.gamv = b.dma("gpsimd", b.hrow[0:1, 0:64], g[0:1, 0:64], waits=[("ss", sv)], sem="hrw")

    sv = b.emit("scalar", lambda h: h.activation(out=c["A"], in_=c["A"], func=AF.Exp),
                waits=[("c0", ldv)], inc="ss")
    b.emit("vector", lambda h: h.tensor_scalar_mul(out=c["A"], in0=c["A"], scalar1=-1.0),
           waits=[("ss", sv)], inc="sv")

    wbuf = [b.carve([128, DH]), b.carve([128, DH])]
    qout = [b.carve([128, DH], BF16) for _ in range(4)]
    qb2 = b.carve([128, DH], BF16)
    accs = b.carve([128, 8])
    acc2 = b.carve([128, 8])
    dump = b.carve([128, DH])
    ones = b.carve([128, 1])
    zerocol = b.carve([128, 1])
    b.emit("vector", lambda h: h.memset(ones, 1.0), inc="sv")
    b.emit("vector", lambda h: h.memset(zerocol, 0.0), inc="sv")
    for l in (0, 1):
        for wi in (0, 1):
            wsrc = b.w1T[l] if wi == 0 else b.w2T[l]
            nk = 8 if wi == 0 else 32
            ncols = DH if wi == 0 else D
            w3 = wsrc.rearrange("(k p) o -> k p o", p=128)
            for k in range(nk):
                buf = wbuf[k % 2]
                lv = b.dma("sync", buf[:, 0:ncols], w3[k],
                           waits=[("ss", b.val("ss")), ("sv", b.val("sv"))],
                           sem="wb%d" % (k % 2))
                b.emit("scalar", lambda h, bu=buf, n=ncols, k=k: h.activation(
                    out=dump[:, 0:n], in_=bu[:, 0:n], func=AF.Abs,
                    accum_out=accs[:, k % 8:k % 8 + 1]),
                    waits=[("wb%d" % (k % 2), lv)] + ([("sv", b.val("sv"))]
                                                      if k % 8 == 0 else []),
                    inc="ss")
                if k % 8 == 7:
                    grp = k // 8
                    b.emit("vector", lambda h, grp=grp: h.tensor_reduce(
                        out=acc2[:, grp:grp + 1], in_=accs[:, 0:8], axis=AXX,
                        op=OP.add), waits=[("ss", b.val("ss"))], inc="sv")
            vv0 = b.val("sv")
            ngr = nk // 8
            ge = b.emit("gpsimd", lambda h, n=ngr: h.tensor_scalar(
                out=acc2[:, 0:n], in0=acc2[:, 0:n], scalar1=0.0, scalar2=None,
                op0=OP.add), waits=[("sv", vv0)], inc="sg")
            gvv = b.emit("vector", lambda h, n=ngr: h.tensor_reduce(
                out=acc2[:, 4:5], in_=acc2[:, 0:n], axis=AXX, op=OP.add),
                waits=[("sg", ge)], inc="sv")
            ge2 = b.emit("gpsimd", lambda h: h.tensor_scalar(
                out=acc2[:, 4:5], in0=acc2[:, 4:5], scalar1=0.0, scalar2=None,
                op0=OP.add), waits=[("sv", gvv)], inc="sg")
            tv = b.emit("tensor", lambda h: h.matmul(
                b.ps[5][0:1, 0:1], acc2[:, 4:5], ones, start=True, stop=True),
                waits=[("sg", ge2), b.bank_free[5]], inc="st_")
            r = c["wrow"]
            cpv = b.emit("vector", lambda h: h.tensor_copy(out=r[0:1, 0:1],
                                                           in_=b.ps[5][0:1, 0:1]),
                         waits=[("st_", tv), ("dma_gpsimd", b.val("dma_gpsimd"))],
                         inc="sv")
            b.free_bank(5, "sv", cpv)
            nel = float(DH * D)
            b.emit("gpsimd", lambda h, n=nel: h.tensor_scalar(
                out=r[0:1, 1:2], in0=r[0:1, 0:1], scalar1=1.0 / n, scalar2=1e-5,
                op0=OP.mult, op1=OP.max), waits=[("sv", cpv)], inc="sg")
            b.emit("gpsimd", lambda h: h.tensor_scalar(
                out=r[0:1, 2:3], in0=r[0:1, 1:2], scalar1=0.5, scalar2=None,
                op0=OP.mult), inc="sg")
            b.emit("gpsimd", lambda h: h.tensor_scalar(
                out=r[0:1, 3:4], in0=r[0:1, 2:3], scalar1=-1.0, scalar2=None,
                op0=OP.mult), inc="sg")
            sv2 = b.emit("gpsimd", lambda h: h.tensor_scalar(
                out=r[0:1, 4:5], in0=r[0:1, 1:2], scalar1=1.0 / 127.0, scalar2=None,
                op0=OP.mult), inc="sg")
            hof = 64 + 5 * (2 * l + wi)
            wv2 = b.dma("gpsimd", b.hrow[0:1, hof:hof + 5], r[0:1, 0:5],
                        waits=[("sg", sv2)], sem="hrw")
            base = 6 * l + 3 * wi
            for j, soff in enumerate((hof + 2, hof + 3, hof + 4)):
                src = bass.AP(b.hrow.tensor, soff, [[0, 128], [1, 1]])
                b.dma("sync", c["wcol"][:, base + j:base + j + 1], src,
                      waits=[("hrw", wv2)], sem="c0")
            cthr = b.val("c0")
            dst4 = b.wq1_s[l] if wi == 0 else b.wq2_s[l]
            nm = 32 if wi == 0 else 8
            for k in range(nk):
                buf = wbuf[k % 2]
                qb = qout[k % 4]
                lv = b.dma("sync", buf[:, 0:ncols], w3[k],
                           waits=[("sv", b.val("sv")), ("c0", cthr)],
                           sem="wb%d" % (k % 2))
                g1 = b.emit("vector", lambda h, bu=buf, n=ncols, bs=base, qb=qb:
                            h.tensor_scalar(
                    out=qb[:, 0:n], in0=bu[:, 0:n],
                    scalar1=b.wcol(bs), scalar2=None, op0=OP.is_ge),
                    waits=[("wb%d" % (k % 2), lv),
                           ("dma_scalar", b.val("dma_scalar"))], inc="sv")
                g2 = b.emit("vector", lambda h, bu=buf, n=ncols, bs=base: h.tensor_scalar(
                    out=qb2[:, 0:n], in0=bu[:, 0:n],
                    scalar1=b.wcol(bs + 1), scalar2=None, op0=OP.is_le), inc="sv")
                sv3 = b.emit("vector", lambda h, n=ncols, qb=qb: h.tensor_sub(
                    out=qb[:, 0:n], in0=qb[:, 0:n], in1=qb2[:, 0:n]), inc="sv")
                for m in range(nm):
                    b.dma("scalar", dst4[m, k], qb[:, m * 128:(m + 1) * 128],
                          waits=[("sv", sv3)])
    b.barrier()
    b.reset_arena("a")


def ph_build_gamma_tiles(b: Bld):
    c = b.c
    b.MT = b.carve([128, H * C0], F32, "w")
    b.gp_rep = b.carve([128, 8 * C0], F32, "w")
    b.gw_rep = b.carve([128, H * HD], F32, "w")
    b.gC_col = b.carve([128, 8], F32, "w")
    b.al_col = b.carve([128, 8], F32, "w")
    b.lg128 = b.carve([128, 16], F32, "w")
    diffsT = b.carve([C0, C0], F32, "w")
    causT = b.carve([C0, C0], F32, "w")
    b.dma("sync", diffsT, b.din["diffsT"], sem="c0")
    b.dma("sync", causT, b.din["causT"], sem="c0")
    for hh in range(16):
        src = bass.AP(b.hrow.tensor, 16 + hh, [[0, 128], [1, 1]])
        b.dma("sync", b.lg128[:, hh:hh + 1], src, waits=[("hrw", b.gamv)], sem="c0")
    for pc in range(8):
        for dst, off in ((b.gC_col, 32), (b.al_col, 48)):
            for half in range(2):
                src = bass.AP(b.hrow.tensor, off + 2 * pc + half, [[0, 64], [1, 1]])
                b.dma("sync", dst[64 * half:64 * half + 64, pc:pc + 1], src,
                      waits=[("hrw", b.gamv)], sem="c0")
    dv = b.val("c0")
    for hh in range(16):
        mt = b.MT[:, hh * C0:(hh + 1) * C0]
        s1 = b.emit("scalar", lambda h, mt=mt, hh=hh, dT=diffsT: h.activation(
            out=mt, in_=dT, func=AF.Exp, scale=b.lg128[:, hh:hh + 1]),
            waits=[("c0", dv)], inc="ss")
        b.emit("vector", lambda h, mt=mt, cT=causT: h.tensor_mul(
            out=mt, in0=mt, in1=cT), waits=[("ss", s1)], inc="sv")
        gw = b.gw_rep[:, hh * HD:(hh + 1) * HD]
        b.emit("scalar", lambda h, gw=gw, hh=hh: h.activation(
            out=gw, in_=c["crampr"], func=AF.Exp, scale=b.lg128[:, hh:hh + 1]),
            waits=[("c0", dv)], inc="ss")
    for pc in range(8):
        gp = b.gp_rep[:, pc * C0:(pc + 1) * C0]
        b.emit("scalar", lambda h, gp=gp, pc=pc: h.activation(
            out=gp, in_=c["cramp"], func=AF.Exp,
            scale=make_pair_col(b, pc)), waits=[("c0", dv)], inc="ss")


def make_pair_col(b: Bld, pc):
    return b.lgpair[:, pc:pc + 1]


def ph_l0a(b: Bld, seq):
    c = b.c
    b.reset_arena("a")
    b.reset_arena("w")
    b.lgpair = b.carve([128, 8], F32, "w")
    for pc in range(8):
        for half in range(2):
            src = bass.AP(b.hrow.tensor, 16 + 2 * pc + half, [[0, 64], [1, 1]])
            b.dma("sync", b.lgpair[64 * half:64 * half + 64, pc:pc + 1], src,
                  waits=[("hrw", b.gamv)], sem="c0")
    ph_build_gamma_tiles(b)
    Ww = [b.carve([128, D]) for _ in range(8)]
    for k in range(8):
        b.dma("sync", Ww[k], b.WwT[k * 128:(k + 1) * 128, :], sem="wres")
    wwv = b.val("wres")
    stA = [b.carve([128, 128]) for _ in range(8)]
    stB = [b.carve([128, 128]) for _ in range(8)]
    vtm = b.carve([128, D])
    vgwA = b.carve([128, 128])
    vgwB = b.carve([128, 128])
    vtmA = b.carve([128, 128])
    vtmB = b.carve([128, 128])
    wkA = b.carve([128, 128])
    wkB = b.carve([128, 128])
    wtA = b.carve([128, 128])
    wtB = b.carve([128, 128])
    smtA = b.carve([128, 128])
    smtB = b.carve([128, 128])
    rkgA = b.carve([128, 128])
    rkgB = b.carve([128, 128])
    rbuf = b.carve([128, 128])
    for tl in (wkA, rkgA):
        b.emit("vector", lambda h, t=tl: h.memset(t[64:128, :], 0.0), inc="sv")
    for tl in (wkB, rkgB):
        b.emit("vector", lambda h, t=tl: h.memset(t[0:64, :], 0.0), inc="sv")
    for tl in (vgwA, vtmA, vgwB, vtmB):
        b.emit("vector", lambda h, t=tl: h.memset(t, 0.0), inc="sv")
    for pc in range(8):
        b.emit("vector", lambda h, p=pc: h.memset(stA[p], 0.0), inc="sv")
        b.emit("vector", lambda h, p=pc: h.memset(stB[p], 0.0), inc="sv")
    z0 = b.val("sv")

    for ch in range(NC0):
        tc = slice(ch * C0, (ch + 1) * C0)
        tcm = slice(max(ch * C0 - 1, 0), ch * C0 + C0 - 1)
        for half in range(2):
            for k in range(8):
                b.emit("tensor", lambda h, k=k, half=half, tc=tc: h.matmul(
                    b.ps[6 + half], b.xr[k][:, tc],
                    Ww[k][:, half * 512:(half + 1) * 512],
                    start=(k == 0), stop=(k == 7)),
                    waits=[("wres", wwv), ("sv", z0), b.bank_free[6 + half]]
                    if k == 0 else (), inc="st_")
        tvv = b.val("st_")
        e1 = b.emit("scalar", lambda h: h.activation(out=vtm[:, 0:512], in_=b.ps[6],
                                                     func=AF.Copy),
                    waits=[("st_", tvv)], inc="ss")
        e2 = b.emit("scalar", lambda h: h.activation(out=vtm[:, 512:1024],
                                                     in_=b.ps[7],
                                                     func=AF.Copy), inc="ss")
        b.free_bank(6, "ss", e2)
        b.free_bank(7, "ss", e2)

        for pc in range(8):
            hA, hB = 2 * pc, 2 * pc + 1
            if ch == 0:
                b.emit("vector", lambda h, pc=pc: h.memset(wkA[0:64, 0:1], 0.0),
                       inc="sv")
                b.emit("vector", lambda h, pc=pc: h.memset(wkB[64:128, 0:1], 0.0),
                       inc="sv")
                v1 = b.emit("vector", lambda h, pc=pc: h.tensor_copy(
                    out=wkA[0:64, 1:128], in_=b.xr[pc][0:64, 0:127]), inc="sv")
                v1 = b.emit("vector", lambda h, pc=pc: h.tensor_copy(
                    out=wkB[64:128, 1:128], in_=b.xr[pc][64:128, 0:127]), inc="sv")
            else:
                v1 = b.emit("vector", lambda h, pc=pc, tcm=tcm: h.tensor_copy(
                    out=wkA[0:64, :], in_=b.xr[pc][0:64, tcm]), inc="sv")
                v1 = b.emit("vector", lambda h, pc=pc, tcm=tcm: h.tensor_copy(
                    out=wkB[64:128, :], in_=b.xr[pc][64:128, tcm]), inc="sv")
            v2 = b.emit("vector", lambda h, pc=pc, tc=tc: h.tensor_mul(
                out=rkgA[0:64, :], in0=b.xr[pc][0:64, tc],
                in1=b.gp_rep[0:64, pc * C0:(pc + 1) * C0]), inc="sv")
            v2 = b.emit("vector", lambda h, pc=pc, tc=tc: h.tensor_mul(
                out=rkgB[64:128, :], in0=b.xr[pc][64:128, tc],
                in1=b.gp_rep[64:128, pc * C0:(pc + 1) * C0]), inc="sv")
            v3 = b.emit("vector", lambda h, pc=pc: h.tensor_copy(
                out=vtmA[:, 0:64], in_=vtm[:, pc * 128:pc * 128 + 64]),
                waits=[("ss", e2)], inc="sv")
            v3 = b.emit("vector", lambda h, pc=pc: h.tensor_copy(
                out=vtmB[:, 64:128], in_=vtm[:, pc * 128 + 64:pc * 128 + 128]),
                inc="sv")
            v4 = b.emit("vector", lambda h, pc=pc, hA=hA: h.tensor_mul(
                out=vgwA[:, 0:64], in0=vtmA[:, 0:64],
                in1=b.gw_rep[:, hA * HD:(hA + 1) * HD]), inc="sv")
            v4 = b.emit("vector", lambda h, pc=pc, hB=hB: h.tensor_mul(
                out=vgwB[:, 64:128], in0=vtmB[:, 64:128],
                in1=b.gw_rep[:, hB * HD:(hB + 1) * HD]), inc="sv")
            vpre = b.val("sv")
            t1 = b.emit("tensor", lambda h: h.transpose(
                b.ps[0][:, 0:128], wkA, b.id_t),
                waits=[("sv", vpre), b.bank_free[0]], inc="st_")
            t2 = b.emit("tensor", lambda h: h.transpose(
                b.ps[1][:, 0:128], wkB, b.id_t),
                waits=[b.bank_free[1]], inc="st_")
            v5 = b.emit("vector", lambda h: h.tensor_copy(out=wtA, in_=b.ps[0][:, 0:128]),
                        waits=[("st_", t2)], inc="sv")
            v6 = b.emit("vector", lambda h: h.tensor_copy(out=wtB, in_=b.ps[1][:, 0:128]),
                        inc="sv")
            b.free_bank(0, "sv", v6)
            b.free_bank(1, "sv", v6)
            s1 = b.emit("tensor", lambda h, pc=pc, tc=tc: h.matmul(
                b.ps[2][:, 0:128], wkA, b.xr[pc][:, tc], start=True, stop=True),
                waits=[("sv", vpre), b.bank_free[2]], inc="st_")
            s2 = b.emit("tensor", lambda h, pc=pc, tc=tc: h.matmul(
                b.ps[3][:, 0:128], wkB, b.xr[pc][:, tc], start=True, stop=True),
                waits=[b.bank_free[3]], inc="st_")
            v7 = b.emit("vector", lambda h, hA=hA: h.tensor_mul(
                out=smtA, in0=b.ps[2][:, 0:128], in1=b.MT[:, hA * C0:(hA + 1) * C0]),
                waits=[("st_", s2)], inc="sv")
            v8 = b.emit("vector", lambda h, hB=hB: h.tensor_mul(
                out=smtB, in0=b.ps[3][:, 0:128], in1=b.MT[:, hB * C0:(hB + 1) * C0]),
                inc="sv")
            b.free_bank(2, "sv", v8)
            b.free_bank(3, "sv", v8)
            o1 = b.emit("tensor", lambda h: h.matmul(
                b.ps[4][:, 0:128], vtmA, smtA, start=True, stop=False),
                waits=[("sv", v8), b.bank_free[4]], inc="st_")
            o2 = b.emit("tensor", lambda h, pc=pc: h.matmul(
                b.ps[4][:, 0:128], stA[pc], rkgA, start=False, stop=False), inc="st_")
            o3 = b.emit("tensor", lambda h: h.matmul(
                b.ps[4][:, 0:128], vtmB, smtB, start=False, stop=False), inc="st_")
            o4 = b.emit("tensor", lambda h, pc=pc: h.matmul(
                b.ps[4][:, 0:128], stB[pc], rkgB, start=False, stop=True), inc="st_")
            r1 = b.emit("scalar", lambda h, pc=pc: h.activation(
                out=rbuf, in_=b.ps[4][:, 0:128], func=AF.Copy,
                scale=b.al_col[:, pc:pc + 1]),
                waits=[("st_", o4), ("dma_scalar", b.val("dma_scalar"))], inc="ss")
            b.free_bank(4, "ss", r1)
            b.dma("scalar", b.reads_s[pc * 128:(pc + 1) * 128, tc], rbuf,
                  waits=[("ss", r1)])
            w1 = b.emit("tensor", lambda h: h.matmul(
                b.ps[5][:, 0:128], wtA, vgwA, start=True, stop=False),
                waits=[("sv", v8), b.bank_free[5]], inc="st_")
            w2 = b.emit("tensor", lambda h: h.matmul(
                b.ps[5][:, 0:128], wtB, vgwB, start=False, stop=True), inc="st_")
            u1 = b.emit("vector", lambda h, pc=pc: h.scalar_tensor_tensor(
                out=stA[pc][0:64, 0:64], in0=stA[pc][0:64, 0:64],
                scalar=b.gC_col[0:64, pc:pc + 1], in1=b.ps[5][0:64, 0:64],
                op0=OP.mult, op1=OP.add), waits=[("st_", w2)], inc="sv")
            u2 = b.emit("vector", lambda h, pc=pc: h.scalar_tensor_tensor(
                out=stB[pc][64:128, 64:128], in0=stB[pc][64:128, 64:128],
                scalar=b.gC_col[64:128, pc:pc + 1], in1=b.ps[5][64:128, 64:128],
                op0=OP.mult, op1=OP.add), inc="sv")
            b.free_bank(5, "sv", u2)

    b.barrier()
    b.reset_arena("a")
    Wr = [[b.carve([128, 128]) for _ in range(8)] for _ in range(8)]
    for k in range(8):
        for mo in range(8):
            b.dma("sync", Wr[k][mo], b.WrT_t[k, mo], sem="wres")
    rtile = [b.carve([128, 512]) for _ in range(2)]
    wrv = b.val("wres")
    for tcq in range(4):
        for k in range(8):
            rl = b.dma("sync", rtile[k % 2],
                       b.reads_s[k * 128:(k + 1) * 128, tcq * 512:(tcq + 1) * 512],
                       waits=([("st_", b.val("st_"))] if (tcq > 0 or k >= 2) else [])
                       + [("dma_scalar", b.val("dma_scalar"))],
                       sem="rt%d" % (k % 2))
            for mo in range(8):
                b.emit("tensor", lambda h, k=k, mo=mo, tcq=tcq, rl=rl: h.matmul(
                    b.ps[mo], Wr[k][mo], rtile[k % 2], start=(k == 0), stop=(k == 7)),
                    ([("wres", wrv)] if k == 0 and mo == 0 else [])
                    + ([("rt%d" % (k % 2), rl)] if mo == 0 else [])
                    + ([b.bank_free[mo]] if k == 0 else []),
                    inc="st_")
        tv = b.val("st_")
        for mo in range(8):
            vv = b.emit("vector", lambda h, mo=mo, tcq=tcq: h.scalar_tensor_tensor(
                out=b.xr[mo][:, tcq * 512:(tcq + 1) * 512],
                in0=b.xr[mo][:, tcq * 512:(tcq + 1) * 512], scalar=2.0,
                in1=b.ps[mo], op0=OP.mult, op1=OP.add),
                waits=[("st_", tv)], inc="sv")
            b.free_bank(mo, "sv", vv)
    b.barrier()
    b.reset_arena("a")
    b.reset_arena("w")


def ph_ln(b: Bld):
    c = b.c
    st = c["sstat"]
    dump = b.carve([128, 1024])
    for tb in range(16):
        for i in range(8):
            b.emit("tensor", lambda h, i=i, tb=tb: h.transpose(
                b.pbig[:, i * 128:(i + 1) * 128],
                b.xr[i][:, tb * 128:(tb + 1) * 128], b.id_t),
                waits=[b.bank_free[6], b.bank_free[7], ("sv", b.val("sv"))]
                if i == 0 else (), inc="st_")
        tv = b.val("st_")
        rv = b.emit("vector", lambda h, tb=tb: h.tensor_reduce(
            out=st[:, tb:tb + 1], in_=b.pbig, axis=AXX, op=OP.add),
            waits=[("st_", tv)], inc="sv")
        s1 = b.emit("scalar", lambda h, tb=tb: h.activation(
            out=dump, in_=b.pbig, func=AF.Square, accum_out=st[:, 16 + tb:17 + tb]),
            waits=[("st_", tv), ("sv", rv)], inc="ss")
        b.free_bank(6, "ss", s1)
        b.free_bank(7, "ss", s1)
    svv, ssv = b.val("sv"), b.val("ss")
    b.emit("gpsimd", lambda h: h.tensor_scalar(
        out=st[:, 0:16], in0=st[:, 0:16], scalar1=1.0 / D, scalar2=None, op0=OP.mult),
        waits=[("sv", svv), ("ss", ssv)], inc="sg")
    gv = b.emit("gpsimd", lambda h: h.tensor_scalar(
        out=st[:, 16:32], in0=st[:, 16:32], scalar1=1.0 / D, scalar2=None,
        op0=OP.mult), inc="sg")
    sq = b.emit("scalar", lambda h: h.activation(out=st[:, 32:48], in_=st[:, 0:16],
                                                 func=AF.Square),
                waits=[("sg", gv)], inc="ss")
    gv2 = b.emit("vector", lambda h: h.tensor_sub(out=st[:, 16:32], in0=st[:, 16:32],
                                                  in1=st[:, 32:48]),
                 waits=[("ss", sq)], inc="sv")
    s1 = b.emit("scalar", lambda h: h.activation(out=st[:, 16:32], in_=st[:, 16:32],
                                                 func=AF.Ln, bias=b.fc(1e-5)),
                waits=[("sv", gv2)], inc="ss")
    s2 = b.emit("scalar", lambda h: h.activation(out=st[:, 16:32], in_=st[:, 16:32],
                                                 func=AF.Exp, scale=b.fc(-0.5)),
                inc="ss")
    w1 = b.bcast_cols_to_row(st[:, 0:16], 16, 4, 0,
                             waits=[("sg", gv), ("ss", s2)])
    w2 = b.bcast_cols_to_row(st[:, 16:32], 16, 4, 2048, waits=[("ss", s2)])
    mrep = b.carve([128, 2 * T])
    rv = b.bcast_read(mrep, 0, 2 * T, max(w1, w2))
    for i in range(8):
        b.emit("vector", lambda h, i=i: h.tensor_sub(
            out=b.xr[i], in0=b.xr[i], in1=mrep[:, 0:T]),
            waits=[("bcr", rv)] if i == 0 else (), inc="sv")
        b.emit("vector", lambda h, i=i: h.tensor_mul(
            out=b.xr[i], in0=b.xr[i], in1=mrep[:, T:2 * T]), inc="sv")
    b.barrier()
    b.reset_arena("a")


def ph_ffn(b: Bld, l):
    c = b.c
    wbase = 6 * l
    st = c["sstat"]
    for tb in range(16):
        for i in range(8):
            b.emit("tensor", lambda h, i=i, tb=tb: h.transpose(
                b.pbig[:, i * 128:(i + 1) * 128],
                b.xr[i][:, tb * 128:(tb + 1) * 128], b.id_t),
                waits=[b.bank_free[6], b.bank_free[7]] if i == 0 else (), inc="st_")
        tv = b.val("st_")
        rv = b.emit("vector", lambda h, tb=tb: h.tensor_reduce(
            out=st[:, tb:tb + 1], in_=b.pbig, axis=AXX, op=OP.max,
            apply_absolute_value=True), waits=[("st_", tv)], inc="sv")
        b.free_bank(6, "sv", rv)
        b.free_bank(7, "sv", rv)
    svv = b.val("sv")
    b.emit("gpsimd", lambda h: h.tensor_scalar(
        out=st[:, 0:16], in0=st[:, 0:16], scalar1=1e-5, scalar2=None, op0=OP.max),
        waits=[("sv", svv)], inc="sg")
    b.emit("gpsimd", lambda h, wb=wbase: h.tensor_scalar(
        out=st[:, 16:32], in0=st[:, 0:16], scalar1=b.wcol(wb + 2), scalar2=None,
        op0=OP.mult), inc="sg")
    gv0 = b.emit("gpsimd", lambda h: h.tensor_scalar(
        out=st[:, 0:16], in0=st[:, 0:16], scalar1=1.0 / 127.0, scalar2=None,
        op0=OP.mult), inc="sg")
    rcv = b.emit("vector", lambda h: h.reciprocal(out=st[:, 0:16], in_=st[:, 0:16]),
                 waits=[("sg", gv0)], inc="sv")
    w1v = b.bcast_cols_to_row(st[:, 0:16], 16, 4, 0, waits=[("sv", rcv)])
    w2v = b.bcast_cols_to_row(st[:, 16:32], 16, 4, 2048, waits=[("sv", rcv)])
    srep = b.carve([128, 2 * T])
    rv = b.bcast_read(srep, 0, 2 * T, max(w1v, w2v))
    scale_rep, si_rep = srep[:, 0:T], srep[:, T:2 * T]

    hs_off = []
    hs = []
    for _ in range(32):
        off = (b.mark("a") + 63) // 64 * 64
        hs.append(b.carve([128, TT]))
        hs_off.append(off)
    hq = [b.carve([128, TT], BF16, "a", at=hs_off[m]) for m in range(32)]
    hrep = b.carve([128, 2 * TT])
    xq = [b.carve([128, TT + 2], BF16) for _ in range(8)]
    itmp = b.carve([128, TT + 2], I32)
    hbuf = b.carve([128, TT + 2])
    cbuf = b.carve([128, TT])
    w1b = [b.carve([128, 8, 128], BF16, "w") for _ in range(2)]
    w2b = [b.carve([128, 8, 128], BF16, "w") for _ in range(2)]

    for t in range(NTT):
        t0 = t * TT
        lo = max(t0 - 1, 0)
        hi = min(t0 + TT + 1, T)
        ncol = hi - lo
        lpad = 1 - (t0 - lo)
        for i in range(8):
            v1 = b.emit("vector", lambda h, i=i, lo=lo, ncol=ncol: h.tensor_mul(
                out=itmp[:, 0:ncol], in0=b.xr[i][:, lo:lo + ncol],
                in1=scale_rep[:, lo:lo + ncol]),
                waits=[("bcr", rv)] if (t == 0 and i == 0) else (), inc="sv")
            b.emit("vector", lambda h, i=i, ncol=ncol: h.tensor_scalar(
                out=xq[i][:, 0:ncol], in0=itmp[:, 0:ncol], scalar1=-128.0,
                scalar2=127.0, op0=OP.max, op1=OP.min), inc="sv")
        xqv = b.val("sv")
        nmain = min(TT, ncol)
        for mc in range(32):
            wbv = b.dma("sync", w1b[mc % 2], b.wq1_s[l][mc].transpose([1, 0, 2]),
                        waits=[("st_", b.val("st_"))] if (t > 0 or mc >= 2) else (),
                        sem="wA" if mc % 2 == 0 else "wB")
            for k in range(8):
                b.emit("tensor", lambda h, k=k, mc=mc, nmain=nmain: h.matmul(
                    b.ps[0][:, 0:nmain], w1b[mc % 2][:, k, :], xq[k][:, 0:nmain],
                    start=(k == 0), stop=(k == 7)),
                    waits=[("wA" if mc % 2 == 0 else "wB", wbv), ("sv", xqv),
                           b.bank_free[0]]
                    if k == 0 else (), inc="st_")
                b.emit("tensor", lambda h, k=k, mc=mc, nmain=nmain, ncol=ncol: h.matmul(
                    b.ps[1][:, 0:ncol - nmain], w1b[mc % 2][:, k, :], xq[k][:, nmain:ncol],
                    start=(k == 0), stop=(k == 7)),
                    waits=[b.bank_free[1]] if k == 0 else (), inc="st_")
            tv = b.val("st_")
            if lpad:
                b.emit("vector", lambda h: h.memset(hbuf[:, 0:1], 0.0),
                       waits=[("ss", b.val("ss"))], inc="sv")
            if hi == T:
                b.emit("vector", lambda h: h.memset(hbuf[:, TT + 1:TT + 2], 0.0),
                       waits=[("ss", b.val("ss"))], inc="sv")
            e1 = b.emit("vector", lambda h, lpad=lpad, nmain=nmain, lo=lo: h.tensor_mul(
                out=hbuf[:, lpad:lpad + nmain], in0=b.ps[0][:, 0:nmain],
                in1=si_rep[:, lo:lo + nmain]),
                waits=[("st_", tv), ("ss", b.val("ss"))], inc="sv")
            e2 = b.emit("vector", lambda h, lpad=lpad, nmain=nmain, ncol=ncol, lo=lo:
                        h.tensor_mul(
                out=hbuf[:, lpad + nmain:lpad + ncol], in0=b.ps[1][:, 0:ncol - nmain],
                in1=si_rep[:, lo + nmain:lo + ncol]), inc="sv")
            b.free_bank(0, "sv", e2)
            b.free_bank(1, "sv", e2)
            ge = b.emit("scalar", lambda h: h.activation(
                out=hbuf, in_=hbuf, func=AF.Copy), waits=[("sv", e2)], inc="ss")
            wc = c["f_cw"][:, (l * 32 + mc) * 3:(l * 32 + mc) * 3 + 3]
            bcl = c["f_cb"][:, l * 32 + mc:l * 32 + mc + 1]
            b.emit("vector", lambda h, wc=wc, bcl=bcl: h.tensor_scalar(
                out=cbuf, in0=hbuf[:, 0:TT], scalar1=wc[:, 0:1], scalar2=bcl,
                op0=OP.mult, op1=OP.add),
                waits=[("ss", ge)], inc="sv")
            b.emit("vector", lambda h, wc=wc: h.scalar_tensor_tensor(
                out=cbuf, in0=hbuf[:, 1:TT + 1], scalar=wc[:, 1:2], in1=cbuf,
                op0=OP.mult, op1=OP.add), inc="sv")
            v3 = b.emit("vector", lambda h, wc=wc: h.scalar_tensor_tensor(
                out=cbuf, in0=hbuf[:, 2:TT + 2], scalar=wc[:, 2:3], in1=cbuf,
                op0=OP.mult, op1=OP.add), inc="sv")
            sl_v = b.emit("scalar", lambda h, mc=mc: h.activation(
                out=hs[mc], in_=cbuf, func=AF.Silu), waits=[("sv", v3)], inc="ss")
            if DEBUG_CAP is not None and t == NTT - 1 and mc == 0:
                cap_hb = b.carve([128, TT + 2], F32, "w")
                cap_hs = b.carve([128, TT], F32, "w")
                b.emit("vector", lambda h, o=cap_hb: h.tensor_copy(out=o, in_=hbuf),
                       waits=[("sg", ge)], inc="sv")
                b.emit("vector", lambda h, o=cap_hs: h.tensor_copy(out=o, in_=hs[0]),
                       waits=[("ss", sl_v)], inc="sv")
        hsv = b.val("ss")
        for tb in range(4):
            for gq in range(4):
                for j in range(8):
                    b.emit("tensor", lambda h, gq=gq, j=j, tb=tb: h.transpose(
                        b.pbig[:, j * 128:(j + 1) * 128],
                        hs[gq * 8 + j][:, tb * 128:(tb + 1) * 128], b.id_t),
                        waits=[b.bank_free[6], b.bank_free[7], ("ss", hsv)]
                        if j == 0 else (), inc="st_")
                tv = b.val("st_")
                vv = b.emit("vector", lambda h, gq=gq: h.tensor_reduce(
                    out=st[:, 48 + gq:49 + gq], in_=b.pbig, axis=AXX, op=OP.max,
                    apply_absolute_value=True), waits=[("st_", tv)], inc="sv")
                b.free_bank(6, "sv", vv)
                b.free_bank(7, "sv", vv)
            vv = b.val("sv")
            ge = b.emit("scalar", lambda h: h.activation(
                out=st[:, 48:52], in_=st[:, 48:52], func=AF.Copy),
                waits=[("sv", vv)], inc="ss")
            b.emit("vector", lambda h, tb=tb: h.tensor_reduce(
                out=st[:, 32 + tb:33 + tb], in_=st[:, 48:52], axis=AXX, op=OP.max),
                waits=[("ss", ge)], inc="sv")
        vv = b.val("sv")
        b.emit("gpsimd", lambda h: h.tensor_scalar(
            out=st[:, 32:36], in0=st[:, 32:36], scalar1=1e-5, scalar2=None,
            op0=OP.max), waits=[("sv", vv)], inc="sg")
        b.emit("gpsimd", lambda h, wb=wbase: h.tensor_scalar(
            out=st[:, 36:40], in0=st[:, 32:36], scalar1=b.wcol(wb + 5), scalar2=None,
            op0=OP.mult), inc="sg")
        ge = b.emit("gpsimd", lambda h: h.tensor_scalar(
            out=st[:, 32:36], in0=st[:, 32:36], scalar1=1.0 / 127.0, scalar2=None,
            op0=OP.mult), inc="sg")
        rcv = b.emit("vector", lambda h: h.reciprocal(out=st[:, 32:36],
                                                      in_=st[:, 32:36]),
                     waits=[("sg", ge)], inc="sv")
        wv1 = b.bcast_cols_to_row(st[:, 32:36], 4, 4, 4096, waits=[("sv", rcv)])
        wv2 = b.bcast_cols_to_row(st[:, 36:40], 4, 4, 4608, waits=[("sv", rcv)])
        hrv = b.bcast_read(hrep, 4096, 2 * TT, max(wv1, wv2))
        if DEBUG_CAP is not None and t == NTT - 1:
            cap_pq = b.carve([128, TT + 64], F32, "w")
            b.emit("vector", lambda h, o=cap_pq: h.tensor_copy(
                out=o[:, 0:TT], in_=hs[0]), waits=[("bcr", hrv)], inc="sv")
            b.emit("vector", lambda h, o=cap_pq: h.tensor_copy(
                out=o[:, TT:TT + 8], in_=c["sstat"][:, 32:40]), inc="sv")
        for mc in range(32):
            v1 = b.emit("vector", lambda h, mc=mc: h.tensor_mul(
                out=itmp[:, 0:TT], in0=hs[mc], in1=hrep[:, 0:TT]),
                waits=[("bcr", hrv)] if mc == 0 else (), inc="sv")
            b.emit("vector", lambda h, mc=mc: h.tensor_scalar(
                out=hq[mc], in0=itmp[:, 0:TT], scalar1=-128.0, scalar2=127.0,
                op0=OP.max, op1=OP.min), inc="sv")
        hqv = b.val("sv")
        for mo in range(8):
            bank = 2 + mo % 2
            w2v_ = None
            for k2 in range(32):
                if k2 % 8 == 0:
                    w2v_ = b.dma("sync", w2b[(k2 // 8) % 2],
                                 b.wq2_s[l][mo][k2:k2 + 8].transpose([1, 0, 2]),
                                 waits=[("st_", b.val("st_"))]
                                 if (t > 0 or k2 >= 16 or mo > 0) else (),
                                 sem="w2A" if (k2 // 8) % 2 == 0 else "w2B")
                b.emit("tensor", lambda h, mo=mo, k2=k2, bank=bank: h.matmul(
                    b.ps[bank], w2b[(k2 // 8) % 2][:, k2 % 8, :], hq[k2],
                    start=(k2 == 0), stop=(k2 == 31)),
                    waits=([("w2A" if (k2 // 8) % 2 == 0 else "w2B", w2v_),
                            ("sv", hqv), b.bank_free[bank]]
                           if k2 == 0 else
                           [("w2A" if (k2 // 8) % 2 == 0 else "w2B", w2v_)])
                    if k2 % 8 == 0 else (),
                    inc="st_")
            tv = b.val("st_")
            v1 = b.emit("vector", lambda h, bank=bank: h.tensor_mul(
                out=cbuf, in0=b.ps[bank], in1=hrep[:, TT:2 * TT]),
                waits=[("st_", tv)], inc="sv")
            ge = b.emit("scalar", lambda h: h.activation(
                out=cbuf, in_=cbuf, func=AF.Copy), waits=[("sv", v1)], inc="ss")
            v2 = b.emit("vector", lambda h, mo=mo, t0=t0: h.tensor_add(
                out=b.xr[mo][:, t0:t0 + TT], in0=b.xr[mo][:, t0:t0 + TT], in1=cbuf),
                waits=[("ss", ge)], inc="sv")
            b.free_bank(bank, "sv", v2)
    b.barrier()
    b.reset_arena("a")
    b.reset_arena("w")


def ph_l1a(b: Bld, seq):
    c = b.c
    b.reset_arena("a")
    b.reset_arena("w")
    wt = [b.carve([128, 8, 128], F32, "w") for _ in range(2)]
    xinb = b.carve([128, T + 3])
    szb = b.carve([128, T])
    cb2 = b.carve([128, T])
    for mc in range(32):
        wv = b.dma("sync", wt[mc % 2], b.WinT_t[mc].transpose([1, 0, 2]),
                   waits=[("st_", b.val("st_"))] if mc >= 2 else (),
                   sem="wA" if mc % 2 == 0 else "wB")
        for k in range(8):
            for tcq in range(4):
                b.emit("tensor", lambda h, k=k, tcq=tcq, mc=mc: h.matmul(
                    b.ps[tcq], wt[mc % 2][:, k, :],
                    b.xr[k][:, tcq * 512:(tcq + 1) * 512],
                    start=(k == 0), stop=(k == 7)),
                    waits=[("wA" if mc % 2 == 0 else "wB", wv)]
                    + [b.bank_free[q] for q in range(4)]
                    if (k == 0 and tcq == 0) else (), inc="st_")
        tv = b.val("st_")
        if mc < 16:
            z1 = b.emit("vector", lambda h: h.memset(xinb[:, 0:3], 0.0),
                        waits=[("dma_scalar", max(0, b.val("dma_scalar") - 64))],
                        inc="sv")
            for tcq in range(4):
                b.emit("scalar", lambda h, tcq=tcq: h.activation(
                    out=xinb[:, 3 + tcq * 512:3 + (tcq + 1) * 512], in_=b.ps[tcq],
                    func=AF.Copy), waits=[("st_", tv), ("sv", z1)], inc="ss")
            sv = b.val("ss")
            wcv = c["m_cw"]
            v1 = b.emit("vector", lambda h, mc=mc: h.tensor_scalar(
                out=cb2, in0=xinb[:, 0:T], scalar1=wcv[:, mc * 4:mc * 4 + 1],
                scalar2=c["m_cb"][:, mc:mc + 1], op0=OP.mult, op1=OP.add),
                waits=[("ss", sv)], inc="sv")
            for kk in (1, 2, 3):
                v1 = b.emit("vector", lambda h, mc=mc, kk=kk: h.scalar_tensor_tensor(
                    out=cb2, in0=xinb[:, kk:kk + T],
                    scalar=wcv[:, mc * 4 + kk:mc * 4 + kk + 1], in1=cb2,
                    op0=OP.mult, op1=OP.add), inc="sv")
            sl = b.emit("scalar", lambda h: h.activation(out=szb, in_=cb2,
                                                         func=AF.Silu),
                        waits=[("sv", v1),
                               ("dma_scalar", b.val("dma_scalar"))], inc="ss")
            for q in range(4):
                b.free_bank(q, "ss", b.val("ss") - 1 + (1 if q == 3 else 1))
            b.dma("scalar", b.xc_s[mc * 128:(mc + 1) * 128, :], szb,
                  waits=[("ss", sl)])
        else:
            for tcq in range(4):
                sl = b.emit("scalar", lambda h, tcq=tcq: h.activation(
                    out=szb[:, tcq * 512:(tcq + 1) * 512], in_=b.ps[tcq],
                    func=AF.Silu),
                    waits=[("st_", tv), ("dma_scalar", b.val("dma_scalar"))]
                    if tcq == 0 else [("st_", tv)], inc="ss")
                b.free_bank(tcq, "ss", sl)
            b.dma("scalar", b.sz_s[(mc - 16) * 128:(mc - 15) * 128, :], szb,
                  waits=[("ss", b.val("ss"))])
    b.barrier()
    b.reset_arena("a")
    b.reset_arena("w")

    wx = b.carve([128, 16 * 96], F32, "w")
    wdt = b.carve([64, 16 * 128], F32, "w")
    wx3 = b.WxT.rearrange("(k p) o -> k p o", p=128)
    for k in range(16):
        b.dma("sync", wx[:, k * 96:(k + 1) * 96], wx3[k], sem="wres")
    for m2 in range(16):
        b.dma("sync", wdt[:, m2 * 128:(m2 + 1) * 128], b.WdtT_t[m2], sem="wres")
    wxv = b.val("wres")
    xctile = [b.carve([128, 512]) for _ in range(2)]
    dtraw = b.carve([64, 512])
    bstg = b.carve([16, 512])
    cstg = b.carve([16, 512])
    tstg = b.carve([128, 64])
    dtb = b.carve([128, 512])
    u_ = b.carve([128, 512])
    for tcq in range(4):
        for k in range(16):
            xl = b.dma("sync", xctile[k % 2],
                       b.xc_s[k * 128:(k + 1) * 128, tcq * 512:(tcq + 1) * 512],
                       waits=[("st_", b.val("st_"))] if (tcq > 0 or k >= 2) else (),
                       sem="rt%d" % (k % 2))
            b.emit("tensor", lambda h, k=k, xl=xl: h.matmul(
                b.ps[0][0:64, :], wx[:, k * 96:k * 96 + 64], xctile[k % 2],
                start=(k == 0), stop=(k == 15)),
                waits=[("wres", wxv), ("rt%d" % (k % 2), xl), b.bank_free[0]]
                if k == 0 else [("rt%d" % (k % 2), xl)], inc="st_")
            b.emit("tensor", lambda h, k=k: h.matmul(
                b.ps[1][0:16, :], wx[:, k * 96 + 64:k * 96 + 80], xctile[k % 2],
                start=(k == 0), stop=(k == 15)),
                waits=[b.bank_free[1]] if k == 0 else (), inc="st_")
            b.emit("tensor", lambda h, k=k: h.matmul(
                b.ps[2][0:16, :], wx[:, k * 96 + 80:k * 96 + 96], xctile[k % 2],
                start=(k == 0), stop=(k == 15)),
                waits=[b.bank_free[2]] if k == 0 else (), inc="st_")
        tv = b.val("st_")
        ev = b.emit("scalar", lambda h: h.activation(out=dtraw, in_=b.ps[0][0:64, :],
                                                     func=AF.Copy),
                    waits=[("st_", tv)], inc="ss")
        b.free_bank(0, "ss", ev)
        bv = b.emit("vector", lambda h: h.tensor_copy(out=bstg, in_=b.ps[1][0:16, :]),
                    waits=[("st_", tv), ("bcw", b.val("bcw"))], inc="sv")
        cv = b.emit("vector", lambda h: h.tensor_copy(out=cstg, in_=b.ps[2][0:16, :]),
                    inc="sv")
        b.free_bank(1, "sv", cv)
        b.free_bank(2, "sv", cv)
        for src, dst in ((bstg, b.brep_s), (cstg, b.crep_s)):
            dstap = bass.AP(dst.tensor, tcq * 512, [[T, 16], [1, 512]])
            b.dma("gpsimd", dstap, src[0:16, :], waits=[("sv", cv)], sem="bcw")
        for m2 in range(16):
            bank = 4 + m2 % 2
            mv = b.emit("tensor", lambda h, m2=m2, bank=bank: h.matmul(
                b.ps[bank], wdt[:, m2 * 128:(m2 + 1) * 128], dtraw,
                start=True, stop=True),
                waits=[("ss", ev), b.bank_free[bank]], inc="st_")
            bcol = c["b_dt"][:, m2:m2 + 1]
            s1 = b.emit("scalar", lambda h, bank=bank, bcol=bcol: h.activation(
                out=dtb, in_=b.ps[bank], func=AF.Abs, bias=bcol),
                waits=[("st_", mv), ("dma_scalar", b.val("dma_scalar"))], inc="ss")
            s2 = b.emit("scalar", lambda h: h.activation(
                out=dtb, in_=dtb, func=AF.Exp, scale=b.fc(-1.0)), inc="ss")
            s3 = b.emit("scalar", lambda h: h.activation(
                out=dtb, in_=dtb, func=AF.Ln, bias=b.fc(1.0)), inc="ss")
            s4 = b.emit("scalar", lambda h, bank=bank, bcol=bcol: h.activation(
                out=u_, in_=b.ps[bank], func=AF.Relu, bias=bcol), inc="ss")
            b.free_bank(bank, "ss", s4)
            v1 = b.emit("vector", lambda h: h.scalar_tensor_tensor(
                out=dtb, in0=u_, scalar=1e-4, in1=dtb, op0=OP.add, op1=OP.add),
                waits=[("ss", s4)], inc="sv")
            b.dma("scalar", b.dtm_s[m2 * 128:(m2 + 1) * 128,
                                    tcq * 512:(tcq + 1) * 512], dtb,
                  waits=[("sv", v1)])
    b.barrier()
    b.reset_arena("a")
    b.reset_arena("w")

    zz = b.emit("vector", lambda h: h.memset(c["scanst"], 0.0), inc="sv")
    dA = [b.carve([128, TS * DS]) for _ in range(2)]
    dBu = b.carve([128, TS * DS])
    ybuf = [b.carve([128, TS]) for _ in range(16)]
    xck = [b.carve([128, TS]) for _ in range(3)]
    dtk = [b.carve([128, TS]) for _ in range(3)]
    szk = [b.carve([128, TS]) for _ in range(3)]
    rrep = b.carve([128, TS])
    Brep = b.carve([128, TS * DS], F32, "w")
    Crep = b.carve([128, TS * DS])
    sdump = b.carve([128, 1024])
    ydt = [None] * 16
    for ch in range(NCH):
        brv = b.dma("sync", Brep, bass.AP(b.brep_s.tensor, ch * TS,
                                          [[0, 128], [T, DS], [1, TS]]),
                    waits=[("sv", b.val("sv"))] if ch > 0 else (), sem="bcr")
        crv = b.dma("sync", Crep, bass.AP(b.crep_s.tensor, ch * TS,
                                          [[0, 128], [T, DS], [1, TS]]), sem="bcr")
        for k in range(16):
            sl = k % 3
            xl = b.dma("sync", xck[sl], b.xc_s[k * 128:(k + 1) * 128,
                                               ch * TS:(ch + 1) * TS],
                       waits=[("sv", b.val("sv")), ("sg", b.val("sg"))]
                       if (ch * 16 + k) >= 3 else (), sem="slx%d" % sl)
            dl = b.dma("sync", dtk[sl], b.dtm_s[k * 128:(k + 1) * 128,
                                                ch * TS:(ch + 1) * TS],
                       sem="sld%d" % sl)
            zl = b.dma("sync", szk[sl], b.sz_s[k * 128:(k + 1) * 128,
                                               ch * TS:(ch + 1) * TS],
                       sem="slz%d" % sl)
            da = dA[k % 2]
            da3 = da.rearrange("p (n t) -> p n t", t=TS)
            dbu3 = dBu.rearrange("p (n t) -> p n t", t=TS)
            B3 = Brep.rearrange("p (n t) -> p n t", t=TS)
            C3 = Crep.rearrange("p (n t) -> p n t", t=TS)
            sv0 = b.val("sv")
            for n in range(DS):
                b.emit("scalar", lambda h, k=k, n=n, da3=da3, sl=sl: h.activation(
                    out=da3[:, n, :], in_=dtk[sl], func=AF.Exp,
                    scale=c["A"][:, k * DS + n:k * DS + n + 1]),
                    waits=[("sld%d" % sl, dl), ("sv", sv0)] if n == 0 else (), inc="ss")
            sev = b.val("ss")
            gv = b.emit("vector", lambda h, da=da: h.tensor_scalar_max(
                out=da, in0=da, scalar1=EXP_NEG10), waits=[("ss", sev)], inc="sv")
            uv = b.emit("vector", lambda h, sl=sl: h.tensor_mul(
                out=dtk[sl], in0=dtk[sl], in1=xck[sl]),
                waits=[("ss", sev), ("slx%d" % sl, xl)], inc="sv")
            for n in range(DS):
                b.emit("vector", lambda h, n=n, sl=sl, dbu3=dbu3, B3=B3:
                       h.tensor_mul(out=dbu3[:, n, :], in0=dtk[sl],
                                    in1=B3[:, n, :]),
                       waits=[("bcr", crv)] if n == 0 else (), inc="sv")
            for n in range(DS):
                b.emit("vector", lambda h, n=n, da3=da3, dbu3=dbu3, k=k:
                       h.tensor_tensor_scan(
                           out=da3[:, n, :], data0=da3[:, n, :], data1=dbu3[:, n, :],
                           initial=c["scanst"][:, k * DS + n:k * DS + n + 1],
                           op0=OP.mult, op1=OP.add), inc="sv")
            scv = b.val("sv")
            b.emit("gpsimd", lambda h, da3=da3, k=k: h.tensor_copy(
                out=c["scanst"][:, k * DS:(k + 1) * DS], in_=da3[:, :, TS - 1]),
                waits=[("sv", scv)], inc="sg")
            b.emit("vector", lambda h, k=k, da3=da3, C3=C3: h.tensor_mul(
                out=ybuf[k], in0=da3[:, 0, :], in1=C3[:, 0, :]),
                waits=[("dma_scalar", b.val("dma_scalar"))], inc="sv")
            for n in range(1, DS):
                b.emit("vector", lambda h, n=n, da3=da3, C3=C3: h.tensor_mul(
                    out=sdump[:, 0:TS], in0=da3[:, n, :], in1=C3[:, n, :]),
                    inc="sv")
                b.emit("vector", lambda h, k=k: h.tensor_add(
                    out=ybuf[k], in0=ybuf[k], in1=sdump[:, 0:TS]), inc="sv")
            b.emit("vector", lambda h, k=k, sl=sl: h.scalar_tensor_tensor(
                out=ybuf[k], in0=xck[sl], scalar=c["D_col"][:, k:k + 1], in1=ybuf[k],
                op0=OP.mult, op1=OP.add), inc="sv")
            ydt[k] = b.emit("vector", lambda h, k=k, sl=sl: h.tensor_mul(
                out=ybuf[k], in0=ybuf[k], in1=szk[sl]),
                waits=[("slz%d" % sl, zl)], inc="sv")
        st = c["sstat"]
        ydall = b.val("sv")
        for blk in range(2):
            for gq in range(2):
                for j in range(8):
                    b.emit("tensor", lambda h, gq=gq, j=j, blk=blk: h.transpose(
                        b.pbig[:, j * 128:(j + 1) * 128],
                        ybuf[gq * 8 + j][:, blk * 128:(blk + 1) * 128], b.id_t),
                        waits=[b.bank_free[6], b.bank_free[7], ("sv", ydall)]
                        if j == 0 else (), inc="st_")
                tv = b.val("st_")
                s1 = b.emit("scalar", lambda h, blk=blk, gq=gq: h.activation(
                    out=sdump, in_=b.pbig, func=AF.Square,
                    accum_out=st[:, 48 + 2 * blk + gq:49 + 2 * blk + gq]),
                    waits=[("st_", tv)], inc="ss")
                b.free_bank(6, "ss", s1)
                b.free_bank(7, "ss", s1)
        ssv = b.val("ss")
        for blk in range(2):
            b.emit("vector", lambda h, blk=blk: h.tensor_add(
                out=st[:, 32 + blk:33 + blk], in0=st[:, 48 + 2 * blk:49 + 2 * blk],
                in1=st[:, 49 + 2 * blk:50 + 2 * blk]),
                waits=[("ss", ssv)] if blk == 0 else (), inc="sv")
        sv = b.emit("vector", lambda h: h.tensor_copy(out=st[:, 48:50],
                                                      in_=st[:, 32:34]), inc="sv")
        s1 = b.emit("scalar", lambda h: h.activation(
            out=st[:, 32:34], in_=st[:, 32:34], func=AF.Ln, scale=b.fc(1.0 / DI),
            bias=b.fc(1e-6)), waits=[("sv", sv)], inc="ss")
        s2 = b.emit("scalar", lambda h: h.activation(
            out=st[:, 32:34], in_=st[:, 32:34], func=AF.Exp, scale=b.fc(-0.5)), inc="ss")
        b.emit("vector", lambda h: h.tensor_copy(out=st[:, 50:51], in_=st[:, 32:33]),
               waits=[("ss", s2)], inc="sv")
        wv = b.bcast_cols_to_row(st[:, 32:34], 2, 4, 6144)
        rv = b.bcast_read(rrep, 6144, TS, wv)
        for k in range(16):
            v1 = b.emit("vector", lambda h, k=k: h.scalar_tensor_tensor(
                out=ybuf[k], in0=ybuf[k], scalar=c["rmsw"][:, k:k + 1], in1=rrep,
                op0=OP.mult, op1=OP.mult),
                waits=[("bcr", rv)] if k == 0 else (), inc="sv")
            b.dma("scalar", b.yp_s[k * 128:(k + 1) * 128, ch * TS:(ch + 1) * TS],
                  ybuf[k], waits=[("sv", v1)])
    b.barrier()
    b.reset_arena("a")
    b.reset_arena("w")

    Wo = [[b.carve([128, 128]) for _ in range(8)] for _ in range(16)]
    for ko in range(16):
        for mo in range(8):
            b.dma("sync", Wo[ko][mo], b.WoT_t[ko, mo], sem="wres")
    ytile = [b.carve([128, 512]) for _ in range(2)]
    wov = b.val("wres")
    for tcq in range(4):
        for ko in range(16):
            yl = b.dma("sync", ytile[ko % 2],
                       b.yp_s[ko * 128:(ko + 1) * 128, tcq * 512:(tcq + 1) * 512],
                       waits=([("st_", b.val("st_"))] if (tcq > 0 or ko >= 2) else [])
                       + [("dma_scalar", b.val("dma_scalar"))],
                       sem="rt%d" % (ko % 2))
            for mo in range(8):
                b.emit("tensor", lambda h, ko=ko, mo=mo, yl=yl: h.matmul(
                    b.ps[mo], Wo[ko][mo], ytile[ko % 2],
                    start=(ko == 0), stop=(ko == 15)),
                    ([("wres", wov)] if ko == 0 and mo == 0 else [])
                    + ([("rt%d" % (ko % 2), yl)] if mo == 0 else [])
                    + ([b.bank_free[mo]] if ko == 0 else []),
                    inc="st_")
        tv = b.val("st_")
        for mo in range(8):
            vv = b.emit("vector", lambda h, mo=mo, tcq=tcq: h.tensor_add(
                out=b.xr[mo][:, tcq * 512:(tcq + 1) * 512],
                in0=b.xr[mo][:, tcq * 512:(tcq + 1) * 512], in1=b.ps[mo]),
                waits=[("st_", tv)], inc="sv")
            b.free_bank(mo, "sv", vv)
    b.barrier()
    b.reset_arena("a")
    b.reset_arena("w")


def build(blocks=BLOCKS):
    b = Bld()
    b.init_consts()
    ph_prelude(b)
    for seq in range(NSEQ):
        for i in range(8):
            b.dma("sync", b.xr[i], b.x_in[seq][i * 128:(i + 1) * 128, :])
        b.barrier()
        for blk in blocks:
            if blk == "l0a":
                ph_l0a(b, seq)
            elif blk == "l0f":
                ph_ffn(b, 0)
            elif blk == "l1a":
                ph_l1a(b, seq)
            elif blk == "l1f":
                ph_ffn(b, 1)
            elif blk == "ln":
                ph_ln(b)
        for i in range(8):
            b.dma("gpsimd", b.out_d[seq][i * 128:(i + 1) * 128, :], b.xr[i])
        b.barrier()
    fv = b.val("dma_gpsimd")
    b.emit("sync", lambda h: h.nop(), waits=[("dma_gpsimd", fv)])
    return b.finalize()


_NC_CACHE = {}


def kernel(**inputs):
    blocks = BLOCKS
    key = tuple(blocks)
    if key not in _NC_CACHE:
        _NC_CACHE[key] = build(blocks)
    nc = _NC_CACHE[key]
    shards, common = host_layout(inputs)
    in_maps = [dict(common, x=s) for s in shards]
    res = run_bass_kernel_spmd(nc, in_maps, core_ids=list(range(N_CORES)))
    outs = []
    for ci in range(N_CORES):
        o = res.results[ci]["out"]
        outs.append(np.ascontiguousarray(o.transpose(0, 2, 1)))
    return np.concatenate(outs, axis=0)
